# version 07
# baseline (speedup 1.0000x reference)
import numpy as np
import concourse.bass as bass
import concourse.mybir as mybir
from concourse.bass_utils import run_bass_kernel_spmd
from concourse.alu_op_type import AluOpType as OP

F32 = mybir.dt.float32
BF16 = mybir.dt.bfloat16
U8 = mybir.dt.uint8
AF = mybir.ActivationFunctionType

NCORES = 8
TPC = 1024
NG = 2
TG = 512
NTT = 4
D = 4096
C = 32
NOB = 8


class Sem:
    def __init__(self, nc, name):
        self.h = nc.alloc_semaphore(name)
        self.n = 0

    def inc(self, instr, k=1):
        instr.then_inc(self.h, k)
        self.n += k


def build_nc():
    nc = bass.Bass("TRN2", target_bir_lowering=False)

    x_in = nc.dram_tensor("x_in", [TPC, D], F32, kind="ExternalInput").ap()
    w1t = nc.dram_tensor("w1t", [NOB, C, 128, 512], F32, kind="ExternalInput").ap()
    b1r = nc.dram_tensor("b1r", [1, D], F32, kind="ExternalInput").ap()
    b2r = nc.dram_tensor("b2r", [1, D], F32, kind="ExternalInput").ap()
    pkd = nc.dram_tensor("pkd", [D, D], U8, kind="ExternalInput").ap()
    sclT_h = nc.dram_tensor("sclT", [64, D], F32, kind="ExternalInput")
    zrT = nc.dram_tensor("zrT", [64, D], F32, kind="ExternalInput").ap()
    onesr = nc.dram_tensor("onesr", [1, 128], F32, kind="ExternalInput").ap()
    selbig = nc.dram_tensor("selbig", [128, 127], BF16, kind="ExternalInput").ap()
    shv = nc.dram_tensor("shv", [128, 1], U8, kind="ExternalInput").ap()
    idb = nc.dram_tensor("idb", [128, 128], BF16, kind="ExternalInput").ap()
    out = nc.dram_tensor("out", [TPC, D], F32, kind="ExternalOutput").ap()
    w1scr = nc.dram_tensor("w1scr", [NOB, C, 2, 128, 512], BF16).ap()
    w2scr = nc.dram_tensor("w2scr", [NOB, C, 2, 128, 512], BF16).ap()

    X = nc.alloc_sbuf_tensor("X", [128, NTT * D], F32)
    XHTH = nc.alloc_sbuf_tensor("XHTH", [128, C * TG], BF16)
    XHTL = nc.alloc_sbuf_tensor("XHTL", [128, C * TG], BF16)
    AR1 = nc.alloc_sbuf_tensor("AR1", [128, D], F32)
    XHTM = AR1[:]
    AR1B = AR1[:].bitcast(BF16)
    WBH = AR1B[:, 0:4096]
    WBL = AR1B[:, 4096:8192]
    AR2 = nc.alloc_sbuf_tensor("AR2", [128, D], F32)
    AR2B = AR2[:].bitcast(BF16)
    XHTMH = AR2B[:, 0:4096]
    XHTML = AR2B[:, 4096:8192]
    FPS = AR2[:, 0:2048]
    SBC = AR2[:, 2048:4096]
    AR3 = nc.alloc_sbuf_tensor("AR3", [128, 2048], F32)
    AR3B = AR3[:].bitcast(BF16)
    AR3U = AR3[:].bitcast(U8)
    PKB = AR3U[:, 0:2048]
    DQN = AR3U[:, 2048:4096]
    DQF = AR3B[:, 2048:4096]

    ZS = nc.alloc_sbuf_tensor("ZS", [64, D], F32)
    HGS = nc.alloc_sbuf_tensor("HGS", [64, TG], F32)
    BR = nc.alloc_sbuf_tensor("BR", [1, D], F32)
    ONES = nc.alloc_sbuf_tensor("ONES", [1, 128], F32)
    SELB = nc.alloc_sbuf_tensor("SELB", [128, 127], BF16)
    SHV = nc.alloc_sbuf_tensor("SHV", [128, 1], U8)
    IDB = nc.alloc_sbuf_tensor("IDB", [128, 128], BF16)
    SUMS = nc.alloc_sbuf_tensor("SUMS", [128, NTT], F32)
    SSQ = nc.alloc_sbuf_tensor("SSQ", [128, NTT], F32)
    M4 = nc.alloc_sbuf_tensor("M4", [128, NTT], F32)
    RR = nc.alloc_sbuf_tensor("RR", [128, NTT], F32)
    RC = nc.alloc_sbuf_tensor("RC", [128, NTT], F32)
    T1 = nc.alloc_sbuf_tensor("T1", [128, NTT], F32)
    T2 = nc.alloc_sbuf_tensor("T2", [128, NTT], F32)
    VARE = nc.alloc_sbuf_tensor("VARE", [128, NTT], F32)

    PS = nc.alloc_psum_tensor("PS", [128, 8 * 512], F32)
    PSB = PS[:].bitcast(BF16)

    def bank(b):
        return PS[:, b * 512:(b + 1) * 512]

    s_lx = Sem(nc, "s_lx")
    s_lw = Sem(nc, "s_lw")
    s_pk = Sem(nc, "s_pk")
    s_sb = Sem(nc, "s_sb")
    s_sw = Sem(nc, "s_sw")
    s_out = Sem(nc, "s_out")
    s_ac = Sem(nc, "s_ac")
    s_nm = Sem(nc, "s_nm")
    s_tp = Sem(nc, "s_tp")
    s_cp = Sem(nc, "s_cp")
    s_mm = Sem(nc, "s_mm")
    s_mmw = Sem(nc, "s_mmw")
    s_ev = Sem(nc, "s_ev")
    s_dqa = Sem(nc, "s_dqa")
    s_dqp = Sem(nc, "s_dqp")
    s_dqb = Sem(nc, "s_dqb")

    with nc.Block("consts") as blk:
        lw0 = s_lw.n

        @blk.sync
        def _(sync):
            for dst, src in [(ONES[:], onesr[:]), (SELB[:], selbig[:]),
                             (SHV[:], shv[:]), (IDB[:], idb[:])]:
                s_lw.inc(sync.dma_start(out=dst, in_=src), 16)
            s_lw.inc(sync.dma_start(out=X[0:64, 0:D], in_=zrT[:]), 16)
            s_lw.inc(sync.dma_start(out=X[0:64, D:2 * D], in_=sclT_h.ap()[:, :]), 16)

        @blk.vector
        def _(vector):
            vector.wait_ge(s_lw.h, lw0 + 96)
            s_nm.inc(vector.tensor_tensor(ZS[:], X[0:64, 0:D],
                                          X[0:64, D:2 * D], op=OP.mult))

    for g in range(NG):
        with nc.Block(f"g{g}_load") as blk:
            outbase = s_out.n

            @blk.sync
            def _(sync, g=g, outbase=outbase):
                if g > 0:
                    sync.wait_ge(s_out.h, outbase)
                for tt in range(NTT):
                    r0 = g * TG + tt * 128
                    s_lx.inc(sync.dma_start(out=X[:, tt * D:(tt + 1) * D],
                                            in_=x_in[r0:r0 + 128, :]), 16)

        for layer in (1, 2):
            with nc.Block(f"g{g}_l{layer}_ln") as blk:
                ac0 = s_ac.n
                nm0 = s_nm.n
                tp0 = s_tp.n
                cp0 = s_cp.n
                mm0 = s_mm.n
                lx0b = s_lx.n - NTT * 16

                @blk.scalar
                def _(scalar, layer=layer, ac0=ac0, nm0=nm0, tp0=tp0, lx0b=lx0b):
                    for tt in range(NTT):
                        if layer == 1:
                            scalar.wait_ge(s_lx.h, lx0b + (tt + 1) * 16)
                        xt = X[:, tt * D:(tt + 1) * D]
                        s_ac.inc(scalar.activation(XHTM, xt, AF.Identity,
                                                   accum_out=SUMS[:, tt:tt + 1]))
                        s_ac.inc(scalar.activation(XHTM, xt, AF.Square,
                                                   accum_out=SSQ[:, tt:tt + 1]))
                    scalar.wait_ge(s_nm.h, nm0 + 1)
                    s_ac.inc(scalar.activation(RR[:], RC[:], AF.Sqrt))
                    for tt in range(NTT):
                        scalar.wait_ge(s_nm.h, nm0 + 3 + 2 * tt)
                        if tt > 0:
                            scalar.wait_ge(s_tp.h, tp0 + tt * 8)
                        s_ac.inc(scalar.copy(XHTMH, XHTM))

                @blk.vector
                def _(vector, layer=layer, ac0=ac0, nm0=nm0, tp0=tp0, cp0=cp0, mm0=mm0):
                    vector.wait_ge(s_ac.h, ac0 + 2 * NTT)
                    vector.tensor_scalar(M4[:], SUMS[:], 1.0 / D, 0.0,
                                         op0=OP.mult, op1=OP.bypass)
                    vector.tensor_scalar(T2[:], SSQ[:], 1.0 / D, 0.0,
                                         op0=OP.mult, op1=OP.bypass)
                    vector.drain()
                    vector.tensor_tensor(T1[:], M4[:], M4[:], op=OP.mult)
                    vector.drain()
                    vector.tensor_tensor(VARE[:], T2[:], T1[:], op=OP.subtract)
                    vector.drain()
                    vector.tensor_scalar(VARE[:], VARE[:], 1e-5, 0.0,
                                         op0=OP.add, op1=OP.bypass)
                    vector.drain()
                    s_nm.inc(vector.reciprocal(RC[:], VARE[:]))
                    vector.wait_ge(s_ac.h, ac0 + 2 * NTT + 1)
                    vector.tensor_tensor(T1[:], RR[:], RR[:], op=OP.mult)
                    vector.drain()
                    vector.tensor_tensor(T1[:], T1[:], VARE[:], op=OP.mult)
                    vector.drain()
                    vector.tensor_scalar(T1[:], T1[:], -0.5, 1.5,
                                         op0=OP.mult, op1=OP.add)
                    vector.drain()
                    s_nm.inc(vector.tensor_tensor(RR[:], RR[:], T1[:], op=OP.mult))
                    vector.drain()
                    for tt in range(NTT):
                        xt = X[:, tt * D:(tt + 1) * D]
                        if tt > 0:
                            vector.wait_ge(s_tp.h, tp0 + tt * 8)
                        s_nm.inc(vector.tensor_scalar(
                            XHTM, xt, M4[:, tt:tt + 1], RR[:, tt:tt + 1],
                            op0=OP.subtract, op1=OP.mult))
                        vector.wait_ge(s_ac.h, ac0 + 2 * NTT + 1 + tt + 1)
                        s_nm.inc(vector.tensor_tensor(XHTML, XHTM, XHTMH,
                                                      op=OP.subtract))
                        for cq in range(8):
                            vector.wait_ge(s_tp.h, tp0 + tt * 8 + cq + 1)
                            half = cq % 2
                            dstbuf = XHTH if cq < 4 else XHTL
                            c0 = (cq % 4) * 8
                            dst = bass.AP(dstbuf, c0 * 512 + tt * 128,
                                          [[C * TG, 128], [512, 8], [1, 128]])
                            src = PSB[:, half * 1024:half * 1024 + 1024]
                            src = src.rearrange("p (a b) -> p a b", b=128)
                            s_cp.inc(vector.tensor_copy(dst, src))
                    if layer == 2:
                        vector.wait_ge(s_mm.h, mm0 + 1)
                        s_cp.inc(vector.tensor_copy(HGS[:], PS[0:64, 1024:1536]))

                @blk.tensor
                def _(tensor, layer=layer, nm0=nm0, cp0=cp0, tp0=tp0, ac0=ac0):
                    for tt in range(NTT):
                        for cq in range(8):
                            if cq == 0:
                                tensor.wait_ge(s_ac.h, ac0 + 2 * NTT + 1 + tt + 1)
                            if cq == 4:
                                tensor.wait_ge(s_nm.h, nm0 + 3 + 2 * tt + 1)
                            if tt * 8 + cq >= 2:
                                tensor.wait_ge(s_cp.h, cp0 + tt * 8 + cq - 1)
                            half = cq % 2
                            srcbuf = XHTMH if cq < 4 else XHTML
                            c0 = (cq % 4) * 8
                            tr = None
                            for j in range(8):
                                dst = PSB[:, half * 1024 + j * 128:half * 1024 + j * 128 + 128]
                                tr = tensor.transpose(
                                    dst, srcbuf[:, (c0 + j) * 128:(c0 + j) * 128 + 128],
                                    IDB[:])
                            s_tp.inc(tr)
                    if layer == 2:
                        tensor.wait_ge(s_cp.h, cp0 + 32)
                        mmi = None
                        for c in range(C):
                            lhs = SELB[:, (63 - 2 * c):(63 - 2 * c) + 64]
                            mmi = tensor.matmul(PS[0:64, 1024:1536], lhs,
                                                XHTH[:, c * 512:(c + 1) * 512],
                                                start=(c == 0), stop=False)
                            mmi = tensor.matmul(PS[0:64, 1024:1536], lhs,
                                                XHTL[:, c * 512:(c + 1) * 512],
                                                start=False, stop=(c == C - 1))
                        s_mm.inc(mmi)

            mode = ("w1g0" if (layer == 1 and g == 0) else
                    "dequant" if (layer == 2 and g == 0) else "scratch")
            scr = w1scr if layer == 1 else w2scr
            with nc.Block(f"g{g}_l{layer}_mm") as blk:
                lw0 = s_lw.n
                mm0 = s_mm.n
                mmw0 = s_mmw.n
                ev0 = s_ev.n
                pk0 = s_pk.n
                sb0 = s_sb.n
                sw0 = s_sw.n
                ac0 = s_ac.n
                dqa0 = s_dqa.n
                dqp0 = s_dqp.n
                dqb0 = s_dqb.n
                sw_l1_total = 256 * 32
                sw_l2_total = 2 * 256 * 32

                @blk.sync
                def _(sync, g=g, layer=layer, mode=mode, lw0=lw0, mmw0=mmw0,
                      ev0=ev0, ac0=ac0, pk0=pk0, sb0=sb0, dqa0=dqa0,
                      dqp0=dqp0, dqb0=dqb0):
                    s_lw.inc(sync.dma_start(
                        out=BR[:], in_=(b1r[:] if layer == 1 else b2r[:])), 16)
                    if mode == "dequant":
                        for idx in range(NOB * C):
                            nb, c = divmod(idx, C)
                            p4 = idx % 4
                            if idx >= 2:
                                sync.wait_ge(s_dqa.h, dqa0 + idx - 1)
                            s_pk.inc(sync.dma_start(
                                out=PKB[:, p4 * 512:(p4 + 1) * 512],
                                in_=pkd[c * 128:(c + 1) * 128,
                                        nb * 512:(nb + 1) * 512]), 16)
                            if idx >= 2:
                                sync.wait_ge(s_dqp.h, dqp0 + idx - 1)
                            src_lo = bass.AP(sclT_h, (2 * c) * D + nb * 512,
                                             [[0, 64], [1, 512]])
                            src_hi = bass.AP(sclT_h, (2 * c + 1) * D + nb * 512,
                                             [[0, 64], [1, 512]])
                            s_sb.inc(sync.dma_start(
                                out=SBC[0:64, p4 * 512:(p4 + 1) * 512],
                                in_=src_lo), 16)
                            s_sb.inc(sync.dma_start(
                                out=SBC[64:128, p4 * 512:(p4 + 1) * 512],
                                in_=src_hi), 16)
                    if mode == "w1g0":
                        for idx in range(NOB * C):
                            ob, c = divmod(idx, C)
                            if idx >= 2:
                                sync.wait_ge(s_dqb.h, dqb0 + idx - 1)
                            s_lw.inc(sync.dma_start(
                                out=FPS[:, (idx % 2) * 512:(idx % 2) * 512 + 512],
                                in_=w1t[ob, c]), 16)
                    elif mode == "scratch":
                        sync.wait_ge(s_sw.h, sw_l1_total if layer == 1 else sw_l2_total)
                        for idx in range(NOB * C):
                            ob, c = divmod(idx, C)
                            slot = idx % 8
                            if idx >= 8:
                                sync.wait_ge(s_mmw.h, mmw0 + (idx - 8) // 4 + 1)
                            s_lw.inc(sync.dma_start(
                                out=WBH[:, slot * 512:(slot + 1) * 512],
                                in_=scr[ob, c, 0]), 16)
                            s_lw.inc(sync.dma_start(
                                out=WBL[:, slot * 512:(slot + 1) * 512],
                                in_=scr[ob, c, 1]), 16)
                    if layer == 2:
                        for ob in range(NOB):
                            for tt in range(NTT):
                                sync.wait_ge(s_ev.h, ev0 + ob * 4 + tt + 1)
                                r0 = g * TG + tt * 128
                                s_out.inc(sync.dma_start(
                                    out=out[r0:r0 + 128, ob * 512:(ob + 1) * 512],
                                    in_=X[:, tt * D + ob * 512:tt * D + ob * 512 + 512]), 16)

                if mode == "dequant":
                    @blk.gpsimd
                    def _(gp, dqb0=dqb0):
                        for widx in range(NOB * C):
                            wnb, wc = divmod(widx, C)
                            slot = widx % 8
                            gp.wait_ge(s_dqb.h, dqb0 + widx + 1)
                            s_sw.inc(gp.dma_start(
                                out=w2scr[wnb, wc, 0],
                                in_=WBH[:, slot * 512:(slot + 1) * 512]), 16)
                            s_sw.inc(gp.dma_start(
                                out=w2scr[wnb, wc, 1],
                                in_=WBL[:, slot * 512:(slot + 1) * 512]), 16)

                    @blk.scalar
                    def _(scalar, dqa0=dqa0, dqp0=dqp0, ac0=ac0, dqb0=dqb0,
                          mmw0=mmw0, sw0=sw0):
                        for idx in range(NOB * C):
                            p4 = idx % 4
                            slot = idx % 8
                            scalar.wait_ge(s_dqa.h, dqa0 + idx + 1)
                            if idx >= 2:
                                scalar.wait_ge(s_dqp.h, dqp0 + idx - 1)
                            s_ac.inc(scalar.copy(DQF[:, p4 * 512:(p4 + 1) * 512],
                                                 DQN[:, p4 * 512:(p4 + 1) * 512]))
                            scalar.wait_ge(s_dqp.h, dqp0 + idx + 1)
                            if idx >= 8:
                                scalar.wait_ge(s_mmw.h, mmw0 + (idx - 8) // 4 + 1)
                                scalar.wait_ge(s_sw.h, sw0 + 32 * (idx - 7))
                            s_ac.inc(scalar.copy(WBH[:, slot * 512:(slot + 1) * 512],
                                                 FPS[:, p4 * 512:(p4 + 1) * 512]))

                if mode == "w1g0":
                    @blk.scalar
                    def _(scalar, lw0=lw0, mmw0=mmw0, sw0=sw0, ac0=ac0):
                        for idx in range(NOB * C):
                            p2 = idx % 2
                            slot = idx % 8
                            scalar.wait_ge(s_lw.h, lw0 + 16 + 16 * (idx + 1))
                            if idx >= 8:
                                scalar.wait_ge(s_mmw.h, mmw0 + (idx - 8) // 4 + 1)
                                scalar.wait_ge(s_sw.h, sw0 + 32 * (idx - 7))
                            s_ac.inc(scalar.copy(WBH[:, slot * 512:(slot + 1) * 512],
                                                 FPS[:, p2 * 512:(p2 + 1) * 512]))

                    @blk.gpsimd
                    def _(gp, dqb0=dqb0, sw0=sw0):
                        for widx in range(NOB * C):
                            wnb, wc = divmod(widx, C)
                            slot = widx % 8
                            gp.wait_ge(s_dqb.h, dqb0 + widx + 1)
                            s_sw.inc(gp.dma_start(
                                out=w1scr[wnb, wc, 0],
                                in_=WBH[:, slot * 512:(slot + 1) * 512]), 16)
                            s_sw.inc(gp.dma_start(
                                out=w1scr[wnb, wc, 1],
                                in_=WBL[:, slot * 512:(slot + 1) * 512]), 16)

                @blk.tensor
                def _(tensor, layer=layer, mode=mode, lw0=lw0, ev0=ev0,
                      dqb0=dqb0, mmw0=mmw0):
                    tensor.wait_ge(s_lw.h, lw0 + 16)
                    for ob in range(NOB):
                        bs = (ob % 2) * 4
                        for c in range(C):
                            idx = ob * C + c
                            slot = idx % 8
                            if c == 0 and ob >= 2:
                                tensor.wait_ge(s_ev.h, ev0 + (ob - 2) * 4 + 4)
                            if mode == "scratch":
                                tensor.wait_ge(s_lw.h, lw0 + 16 + 32 * (idx + 1))
                            else:
                                tensor.wait_ge(s_dqb.h, dqb0 + idx + 1)
                            wh = WBH[:, slot * 512:(slot + 1) * 512]
                            wl = WBL[:, slot * 512:(slot + 1) * 512]
                            mmi = None
                            for tt in range(NTT):
                                xh = XHTH[:, c * 512 + tt * 128:c * 512 + tt * 128 + 128]
                                xl = XHTL[:, c * 512 + tt * 128:c * 512 + tt * 128 + 128]
                                mmi = tensor.matmul(bank(bs + tt), xh, wh,
                                                    start=(c == 0), stop=False)
                                mmi = tensor.matmul(bank(bs + tt), xh, wl,
                                                    start=False, stop=False)
                                mmi = tensor.matmul(bank(bs + tt), xl, wh,
                                                    start=False, stop=False)
                            if c % 4 == 3:
                                s_mmw.inc(mmi)
                        for tt in range(NTT):
                            if layer == 2:
                                tensor.matmul(bank(bs + tt),
                                              HGS[:, tt * 128:tt * 128 + 128],
                                              ZS[:, ob * 512:(ob + 1) * 512],
                                              start=False, stop=False)
                            s_mm.inc(tensor.matmul(bank(bs + tt), ONES[:],
                                                   BR[0:1, ob * 512:(ob + 1) * 512],
                                                   start=False, stop=True))

                @blk.vector
                def _(vector, mode=mode, mm0=mm0, ev0=ev0, pk0=pk0, sb0=sb0,
                      ac0=ac0, dqa0=dqa0, dqp0=dqp0, dqb0=dqb0, sw0=sw0,
                      mmw0=mmw0, lw0=lw0):
                    def emit_evac(ob):
                        bs = (ob % 2) * 4
                        for tt in range(NTT):
                            vector.wait_ge(s_mm.h, mm0 + ob * 4 + tt + 1)
                            dst = X[:, tt * D + ob * 512:tt * D + ob * 512 + 512]
                            s_ev.inc(vector.tensor_tensor(dst, dst, bank(bs + tt),
                                                          op=OP.add))

                    if mode == "scratch":
                        for ob in range(NOB):
                            emit_evac(ob)
                    elif mode == "w1g0":
                        for idx in range(NOB * C):
                            p2 = idx % 2
                            slot = idx % 8
                            vector.wait_ge(s_ac.h, ac0 + idx + 1)
                            s_dqb.inc(vector.tensor_tensor(
                                WBL[:, slot * 512:(slot + 1) * 512],
                                FPS[:, p2 * 512:(p2 + 1) * 512],
                                WBH[:, slot * 512:(slot + 1) * 512],
                                op=OP.subtract))
                            if idx % C == 2 and idx >= C + 2:
                                emit_evac(idx // C - 1)
                        emit_evac(NOB - 1)
                    else:
                        def emit_nib(idx):
                            p4 = idx % 4
                            vector.wait_ge(s_pk.h, pk0 + 16 * (idx + 1))
                            if idx >= 2:
                                vector.wait_ge(s_ac.h, ac0 + 2 * (idx - 2) + 1)
                            s_dqa.inc(vector.tensor_scalar(
                                DQN[:, p4 * 512:(p4 + 1) * 512],
                                PKB[:, p4 * 512:(p4 + 1) * 512],
                                SHV[:], 15,
                                op0=OP.logical_shift_right, op1=OP.bitwise_and))

                        def emit_prodf(idx):
                            p4 = idx % 4
                            vector.wait_ge(s_ac.h, ac0 + 2 * idx + 1)
                            vector.wait_ge(s_sb.h, sb0 + 32 * (idx + 1))
                            if idx >= 2:
                                vector.wait_ge(s_dqb.h, dqb0 + idx - 1)
                            s_dqp.inc(vector.tensor_tensor(
                                FPS[:, p4 * 512:(p4 + 1) * 512],
                                DQF[:, p4 * 512:(p4 + 1) * 512],
                                SBC[:, p4 * 512:(p4 + 1) * 512], op=OP.mult))

                        def emit_prodl(idx):
                            p4 = idx % 4
                            slot = idx % 8
                            vector.wait_ge(s_ac.h, ac0 + 2 * idx + 2)
                            if idx >= 8:
                                vector.wait_ge(s_mmw.h, mmw0 + (idx - 8) // 4 + 1)
                                vector.wait_ge(s_sw.h, sw0 + 32 * (idx - 7))
                            s_dqb.inc(vector.tensor_tensor(
                                WBL[:, slot * 512:(slot + 1) * 512],
                                FPS[:, p4 * 512:(p4 + 1) * 512],
                                WBH[:, slot * 512:(slot + 1) * 512],
                                op=OP.subtract))

                        emit_nib(0)
                        emit_nib(1)
                        for idx in range(NOB * C):
                            emit_prodf(idx)
                            if idx + 2 < NOB * C:
                                emit_nib(idx + 2)
                            emit_prodl(idx)
                            if idx % C == 2 and idx >= C + 2:
                                emit_evac(idx // C - 1)
                        emit_evac(NOB - 1)

    with nc.Block("fin") as blk:
        total_out = s_out.n

        @blk.sync
        def _(sync):
            sync.wait_ge(s_out.h, total_out)

    return nc


def _host_prep(L0_attn_x, L0_attn_weight, L0_attn_bias, L0_ffn_packed_weight,
               L0_ffn_scales, L0_ffn_zeros, L0_ffn_bias):
    x = np.ascontiguousarray(np.asarray(L0_attn_x, dtype=np.float32).reshape(8192, D))
    W1 = np.asarray(L0_attn_weight, dtype=np.float32)
    w1t = np.ascontiguousarray(
        W1.T.reshape(D, NOB, 512).transpose(1, 0, 2).reshape(NOB, C, 128, 512))
    b1r = np.ascontiguousarray(np.asarray(L0_attn_bias, np.float32).reshape(1, D))
    b2r = np.ascontiguousarray(np.asarray(L0_ffn_bias, np.float32).reshape(1, D))
    pk = np.ascontiguousarray(np.asarray(L0_ffn_packed_weight, np.int32))
    pk_u8 = pk.view(np.uint8).reshape(D, D // 2, 4)[:, :, 0]
    pkd = np.ascontiguousarray(np.repeat(pk_u8.T, 2, axis=0))
    sclT = np.ascontiguousarray(np.asarray(L0_ffn_scales, np.float32).T)
    zrT = np.ascontiguousarray(np.asarray(L0_ffn_zeros, np.float32).T)
    onesr = np.ones((1, 128), np.float32)
    import ml_dtypes
    selbig = np.zeros((128, 127), ml_dtypes.bfloat16)
    selbig[0:64, 63] = -1.0
    selbig[64:128, 64] = -1.0
    shv = np.zeros((128, 1), np.uint8)
    shv[1::2] = 4
    idb = np.eye(128, dtype=ml_dtypes.bfloat16)
    shared = dict(w1t=w1t, b1r=b1r, b2r=b2r, pkd=pkd, sclT=sclT, zrT=zrT,
                  onesr=onesr, selbig=selbig.view(np.uint16),
                  shv=shv, idb=idb.view(np.uint16))
    xs = x.reshape(NCORES, TPC, D)
    return xs, shared


def kernel(L0_attn_x, L0_attn_weight, L0_attn_bias, L0_ffn_packed_weight,
           L0_ffn_scales, L0_ffn_zeros, L0_ffn_bias):
    xs, shared = _host_prep(L0_attn_x, L0_attn_weight, L0_attn_bias,
                            L0_ffn_packed_weight, L0_ffn_scales,
                            L0_ffn_zeros, L0_ffn_bias)
    nc = build_nc()
    in_maps = [dict(shared, x_in=np.ascontiguousarray(xs[i])) for i in range(NCORES)]
    res = run_bass_kernel_spmd(nc, in_maps, core_ids=list(range(NCORES)))
    full = np.concatenate([r["out"] for r in res.results], axis=0)
    return full.reshape(1, 8192, D).astype(np.float32)


# version 08
# speedup vs baseline: 1.0436x; 1.0436x over previous
import numpy as np
import concourse.bass as bass
import concourse.mybir as mybir
from concourse.bass_utils import run_bass_kernel_spmd
from concourse.alu_op_type import AluOpType as OP

F32 = mybir.dt.float32
BF16 = mybir.dt.bfloat16
U8 = mybir.dt.uint8
AF = mybir.ActivationFunctionType

NCORES = 8
TPC = 1024
NG = 2
TG = 512
NTT = 4
D = 4096
C = 32
NOB = 8


class Sem:
    def __init__(self, nc, name):
        self.h = nc.alloc_semaphore(name)
        self.n = 0

    def inc(self, instr, k=1):
        instr.then_inc(self.h, k)
        self.n += k


def build_nc():
    nc = bass.Bass("TRN2", target_bir_lowering=False)

    x_in = nc.dram_tensor("x_in", [TPC, D], F32, kind="ExternalInput").ap()
    w1t = nc.dram_tensor("w1t", [NOB, C, 128, 512], F32, kind="ExternalInput").ap()
    b1r = nc.dram_tensor("b1r", [1, D], F32, kind="ExternalInput").ap()
    b2r = nc.dram_tensor("b2r", [1, D], F32, kind="ExternalInput").ap()
    pkd = nc.dram_tensor("pkd", [D, D], U8, kind="ExternalInput").ap()
    sclT_h = nc.dram_tensor("sclT", [64, D], F32, kind="ExternalInput")
    zrT = nc.dram_tensor("zrT", [64, D], F32, kind="ExternalInput").ap()
    onesr = nc.dram_tensor("onesr", [1, 128], BF16, kind="ExternalInput").ap()
    selbig = nc.dram_tensor("selbig", [128, 127], BF16, kind="ExternalInput").ap()
    shv = nc.dram_tensor("shv", [128, 1], U8, kind="ExternalInput").ap()
    idb = nc.dram_tensor("idb", [128, 128], BF16, kind="ExternalInput").ap()
    out = nc.dram_tensor("out", [TPC, D], F32, kind="ExternalOutput").ap()
    w1scr = nc.dram_tensor("w1scr", [NOB, C, 2, 128, 512], BF16).ap()
    w2scr = nc.dram_tensor("w2scr", [NOB, C, 2, 128, 512], BF16).ap()

    X = nc.alloc_sbuf_tensor("X", [128, NTT * D], F32)
    XHTH = nc.alloc_sbuf_tensor("XHTH", [128, C * TG], BF16)
    XHTL = nc.alloc_sbuf_tensor("XHTL", [128, C * TG], BF16)
    AR1 = nc.alloc_sbuf_tensor("AR1", [128, D], F32)
    XHTM = AR1[:]
    AR1B = AR1[:].bitcast(BF16)
    WBH = AR1B[:, 0:4096]
    WBL = AR1B[:, 4096:8192]
    AR2 = nc.alloc_sbuf_tensor("AR2", [128, D], F32)
    AR2B = AR2[:].bitcast(BF16)
    XHTMH = AR2B[:, 0:4096]
    XHTML = AR2B[:, 4096:8192]
    FPS = AR2[:, 0:2048]
    SBC = AR2[:, 2048:4096]
    AR3 = nc.alloc_sbuf_tensor("AR3", [128, 2048], F32)
    AR3B = AR3[:].bitcast(BF16)
    AR3U = AR3[:].bitcast(U8)
    PKB = AR3U[:, 0:2048]
    DQN = AR3U[:, 2048:4096]
    DQF = AR3B[:, 2048:4096]

    ZSH = nc.alloc_sbuf_tensor("ZSH", [65, D], BF16)
    ZSL = nc.alloc_sbuf_tensor("ZSL", [65, D], BF16)
    HGS = nc.alloc_sbuf_tensor("HGS", [64, TG], F32)
    HGSH = nc.alloc_sbuf_tensor("HGSH", [65, TG], BF16)
    HGSL = nc.alloc_sbuf_tensor("HGSL", [65, TG], BF16)
    B1H = nc.alloc_sbuf_tensor("B1H", [1, D], BF16)
    B1L = nc.alloc_sbuf_tensor("B1L", [1, D], BF16)
    ONES = nc.alloc_sbuf_tensor("ONES", [1, 128], BF16)
    SELB = nc.alloc_sbuf_tensor("SELB", [128, 127], BF16)
    SHV = nc.alloc_sbuf_tensor("SHV", [128, 1], U8)
    IDB = nc.alloc_sbuf_tensor("IDB", [128, 128], BF16)
    SUMS = nc.alloc_sbuf_tensor("SUMS", [128, NTT], F32)
    SSQ = nc.alloc_sbuf_tensor("SSQ", [128, NTT], F32)
    M4 = nc.alloc_sbuf_tensor("M4", [128, NTT], F32)
    RR = nc.alloc_sbuf_tensor("RR", [128, NTT], F32)
    RC = nc.alloc_sbuf_tensor("RC", [128, NTT], F32)
    T1 = nc.alloc_sbuf_tensor("T1", [128, NTT], F32)
    T2 = nc.alloc_sbuf_tensor("T2", [128, NTT], F32)
    VARE = nc.alloc_sbuf_tensor("VARE", [128, NTT], F32)

    PS = nc.alloc_psum_tensor("PS", [128, 8 * 512], F32)
    PSB = PS[:].bitcast(BF16)

    def bank(b):
        return PS[:, b * 512:(b + 1) * 512]

    s_lx = Sem(nc, "s_lx")
    s_lw = Sem(nc, "s_lw")
    s_pk = Sem(nc, "s_pk")
    s_sb = Sem(nc, "s_sb")
    s_sw = Sem(nc, "s_sw")
    s_out = Sem(nc, "s_out")
    s_ac = Sem(nc, "s_ac")
    s_nm = Sem(nc, "s_nm")
    s_tp = Sem(nc, "s_tp")
    s_cp = Sem(nc, "s_cp")
    s_mm = Sem(nc, "s_mm")
    s_mmw = Sem(nc, "s_mmw")
    s_ev = Sem(nc, "s_ev")
    s_dqa = Sem(nc, "s_dqa")
    s_dqp = Sem(nc, "s_dqp")
    s_dqb = Sem(nc, "s_dqb")

    with nc.Block("consts") as blk:
        lw0 = s_lw.n

        @blk.sync
        def _(sync):
            for dst, src in [(ONES[:], onesr[:]), (SELB[:], selbig[:]),
                             (SHV[:], shv[:]), (IDB[:], idb[:])]:
                s_lw.inc(sync.dma_start(out=dst, in_=src), 16)
            s_lw.inc(sync.dma_start(out=X[0:64, 0:D], in_=zrT[:]), 16)
            s_lw.inc(sync.dma_start(out=X[0:64, D:2 * D], in_=sclT_h.ap()[:, :]), 16)
            s_lw.inc(sync.dma_start(out=X[0:1, 3 * D:4 * D], in_=b1r[:]), 16)
            s_lw.inc(sync.dma_start(out=X[64:65, 3 * D:4 * D], in_=b2r[:]), 16)

        @blk.vector
        def _(vector):
            vector.wait_ge(s_lw.h, lw0 + 128)
            s_nm.inc(vector.tensor_tensor(X[0:64, 2 * D:3 * D], X[0:64, 0:D],
                                          X[0:64, D:2 * D], op=OP.mult))
            s_nm.inc(vector.tensor_scalar(X[64:65, 3 * D:4 * D],
                                          X[64:65, 3 * D:4 * D], -1.0, 0.0,
                                          op0=OP.mult, op1=OP.bypass))
            vector.memset(HGSH[64:65, :], -1.0)
            vector.memset(HGSL[64:65, :], 0.0)
            vector.wait_ge(s_ac.h, 3)
            s_nm.inc(vector.tensor_tensor(ZSL[0:64, :], X[0:64, 2 * D:3 * D],
                                          ZSH[0:64, :], op=OP.subtract))
            s_nm.inc(vector.tensor_tensor(B1L[:], X[0:1, 3 * D:4 * D],
                                          B1H[:], op=OP.subtract))
            s_nm.inc(vector.tensor_tensor(ZSL[64:65, :], X[64:65, 3 * D:4 * D],
                                          ZSH[64:65, :], op=OP.subtract))

        @blk.scalar
        def _(scalar):
            scalar.wait_ge(s_nm.h, 1)
            s_ac.inc(scalar.copy(ZSH[0:64, :], X[0:64, 2 * D:3 * D]))
            scalar.wait_ge(s_lw.h, lw0 + 128)
            s_ac.inc(scalar.copy(B1H[:], X[0:1, 3 * D:4 * D]))
            scalar.wait_ge(s_nm.h, 2)
            s_ac.inc(scalar.copy(ZSH[64:65, :], X[64:65, 3 * D:4 * D]))

    for g in range(NG):
        with nc.Block(f"g{g}_load") as blk:
            outbase = s_out.n

            @blk.sync
            def _(sync, g=g, outbase=outbase):
                if g > 0:
                    sync.wait_ge(s_out.h, outbase)
                for tt in range(NTT):
                    r0 = g * TG + tt * 128
                    s_lx.inc(sync.dma_start(out=X[:, tt * D:(tt + 1) * D],
                                            in_=x_in[r0:r0 + 128, :]), 16)

        for layer in (1, 2):
            with nc.Block(f"g{g}_l{layer}_ln") as blk:
                ac0 = s_ac.n
                nm0 = s_nm.n
                tp0 = s_tp.n
                cp0 = s_cp.n
                mm0 = s_mm.n
                lw0b = s_lw.n
                lx0b = s_lx.n - NTT * 16


                @blk.scalar
                def _(scalar, layer=layer, ac0=ac0, nm0=nm0, tp0=tp0, lx0b=lx0b,
                      cp0=cp0, lw0b=lw0b):
                    for tt in range(NTT):
                        if layer == 1:
                            scalar.wait_ge(s_lx.h, lx0b + (tt + 1) * 16)
                        xt = X[:, tt * D:(tt + 1) * D]
                        s_ac.inc(scalar.activation(XHTM, xt, AF.Identity,
                                                   accum_out=SUMS[:, tt:tt + 1]))
                        s_ac.inc(scalar.activation(XHTM, xt, AF.Square,
                                                   accum_out=SSQ[:, tt:tt + 1]))
                    scalar.wait_ge(s_nm.h, nm0 + 1)
                    s_ac.inc(scalar.activation(RR[:], RC[:], AF.Sqrt))
                    for tt in range(NTT):
                        scalar.wait_ge(s_nm.h, nm0 + 3 + 2 * tt)
                        if tt > 0:
                            scalar.wait_ge(s_tp.h, tp0 + tt * 8)
                        s_ac.inc(scalar.copy(XHTMH, XHTM))
                    if layer == 2:
                        scalar.wait_ge(s_cp.h, cp0 + 33)
                        s_ac.inc(scalar.copy(HGSH[0:64, :], HGS[:]))

                @blk.vector
                def _(vector, layer=layer, ac0=ac0, nm0=nm0, tp0=tp0, cp0=cp0, mm0=mm0):
                    vector.wait_ge(s_ac.h, ac0 + 2 * NTT)
                    vector.tensor_scalar(M4[:], SUMS[:], 1.0 / D, 0.0,
                                         op0=OP.mult, op1=OP.bypass)
                    vector.tensor_scalar(T2[:], SSQ[:], 1.0 / D, 0.0,
                                         op0=OP.mult, op1=OP.bypass)
                    vector.drain()
                    vector.tensor_tensor(T1[:], M4[:], M4[:], op=OP.mult)
                    vector.drain()
                    vector.tensor_tensor(VARE[:], T2[:], T1[:], op=OP.subtract)
                    vector.drain()
                    vector.tensor_scalar(VARE[:], VARE[:], 1e-5, 0.0,
                                         op0=OP.add, op1=OP.bypass)
                    vector.drain()
                    s_nm.inc(vector.reciprocal(RC[:], VARE[:]))
                    vector.wait_ge(s_ac.h, ac0 + 2 * NTT + 1)
                    vector.tensor_tensor(T1[:], RR[:], RR[:], op=OP.mult)
                    vector.drain()
                    vector.tensor_tensor(T1[:], T1[:], VARE[:], op=OP.mult)
                    vector.drain()
                    vector.tensor_scalar(T1[:], T1[:], -0.5, 1.5,
                                         op0=OP.mult, op1=OP.add)
                    vector.drain()
                    s_nm.inc(vector.tensor_tensor(RR[:], RR[:], T1[:], op=OP.mult))
                    vector.drain()
                    for tt in range(NTT):
                        xt = X[:, tt * D:(tt + 1) * D]
                        if tt > 0:
                            vector.wait_ge(s_tp.h, tp0 + tt * 8)
                        s_nm.inc(vector.tensor_scalar(
                            XHTM, xt, M4[:, tt:tt + 1], RR[:, tt:tt + 1],
                            op0=OP.subtract, op1=OP.mult))
                        vector.wait_ge(s_ac.h, ac0 + 2 * NTT + 1 + tt + 1)
                        s_nm.inc(vector.tensor_tensor(XHTML, XHTM, XHTMH,
                                                      op=OP.subtract))
                        for cq in range(8):
                            vector.wait_ge(s_tp.h, tp0 + tt * 8 + cq + 1)
                            half = cq % 2
                            dstbuf = XHTH if cq < 4 else XHTL
                            c0 = (cq % 4) * 8
                            dst = bass.AP(dstbuf, c0 * 512 + tt * 128,
                                          [[C * TG, 128], [512, 8], [1, 128]])
                            src = PSB[:, half * 1024:half * 1024 + 1024]
                            src = src.rearrange("p (a b) -> p a b", b=128)
                            s_cp.inc(vector.tensor_copy(dst, src))
                    if layer == 2:
                        vector.wait_ge(s_mm.h, mm0 + 1)
                        s_cp.inc(vector.tensor_copy(HGS[:], PS[0:64, 1024:1536]))
                    if layer == 2:
                        nb0 = 2 * NTT + 1 + NTT
                        vector.wait_ge(s_ac.h, ac0 + nb0 + 1)
                        s_nm.inc(vector.tensor_tensor(HGSL[0:64, :], HGS[:],
                                                      HGSH[0:64, :],
                                                      op=OP.subtract))

                @blk.tensor
                def _(tensor, layer=layer, nm0=nm0, cp0=cp0, tp0=tp0, ac0=ac0):
                    for tt in range(NTT):
                        for cq in range(8):
                            if cq == 0:
                                tensor.wait_ge(s_ac.h, ac0 + 2 * NTT + 1 + tt + 1)
                            if cq == 4:
                                tensor.wait_ge(s_nm.h, nm0 + 3 + 2 * tt + 1)
                            if tt * 8 + cq >= 2:
                                tensor.wait_ge(s_cp.h, cp0 + tt * 8 + cq - 1)
                            half = cq % 2
                            srcbuf = XHTMH if cq < 4 else XHTML
                            c0 = (cq % 4) * 8
                            tr = None
                            for j in range(8):
                                dst = PSB[:, half * 1024 + j * 128:half * 1024 + j * 128 + 128]
                                tr = tensor.transpose(
                                    dst, srcbuf[:, (c0 + j) * 128:(c0 + j) * 128 + 128],
                                    IDB[:])
                            s_tp.inc(tr)
                    if layer == 2:
                        tensor.wait_ge(s_cp.h, cp0 + 32)
                        mmi = None
                        for c in range(C):
                            lhs = SELB[:, (63 - 2 * c):(63 - 2 * c) + 64]
                            mmi = tensor.matmul(PS[0:64, 1024:1536], lhs,
                                                XHTH[:, c * 512:(c + 1) * 512],
                                                start=(c == 0), stop=False)
                            mmi = tensor.matmul(PS[0:64, 1024:1536], lhs,
                                                XHTL[:, c * 512:(c + 1) * 512],
                                                start=False, stop=(c == C - 1))
                        s_mm.inc(mmi)

            mode = ("w1g0" if (layer == 1 and g == 0) else
                    "dequant" if (layer == 2 and g == 0) else "scratch")
            scr = w1scr if layer == 1 else w2scr
            with nc.Block(f"g{g}_l{layer}_mm") as blk:
                lw0 = s_lw.n
                mm0 = s_mm.n
                mmw0 = s_mmw.n
                ev0 = s_ev.n
                pk0 = s_pk.n
                sb0 = s_sb.n
                sw0 = s_sw.n
                ac0 = s_ac.n
                dqa0 = s_dqa.n
                dqp0 = s_dqp.n
                dqb0 = s_dqb.n
                sw_l1_total = 256 * 32
                sw_l2_total = 2 * 256 * 32

                @blk.sync
                def _(sync, g=g, layer=layer, mode=mode, lw0=lw0, mmw0=mmw0,
                      ev0=ev0, ac0=ac0, pk0=pk0, sb0=sb0, dqa0=dqa0,
                      dqp0=dqp0, dqb0=dqb0):
                    if mode == "dequant":
                        for idx in range(NOB * C):
                            nb, c = divmod(idx, C)
                            p4 = idx % 4
                            if idx >= 2:
                                sync.wait_ge(s_dqa.h, dqa0 + idx - 1)
                            s_pk.inc(sync.dma_start(
                                out=PKB[:, p4 * 512:(p4 + 1) * 512],
                                in_=pkd[c * 128:(c + 1) * 128,
                                        nb * 512:(nb + 1) * 512]), 16)
                            if idx >= 2:
                                sync.wait_ge(s_dqp.h, dqp0 + idx - 1)
                            src_lo = bass.AP(sclT_h, (2 * c) * D + nb * 512,
                                             [[0, 64], [1, 512]])
                            src_hi = bass.AP(sclT_h, (2 * c + 1) * D + nb * 512,
                                             [[0, 64], [1, 512]])
                            s_sb.inc(sync.dma_start(
                                out=SBC[0:64, p4 * 512:(p4 + 1) * 512],
                                in_=src_lo), 16)
                            s_sb.inc(sync.dma_start(
                                out=SBC[64:128, p4 * 512:(p4 + 1) * 512],
                                in_=src_hi), 16)
                    if mode == "w1g0":
                        for idx in range(NOB * C):
                            ob, c = divmod(idx, C)
                            if idx >= 2:
                                sync.wait_ge(s_dqb.h, dqb0 + idx - 1)
                            s_lw.inc(sync.dma_start(
                                out=FPS[:, (idx % 2) * 512:(idx % 2) * 512 + 512],
                                in_=w1t[ob, c]), 16)
                    elif mode == "scratch":
                        sync.wait_ge(s_sw.h, sw_l1_total if layer == 1 else sw_l2_total)
                        for idx in range(NOB * C):
                            ob, c = divmod(idx, C)
                            slot = idx % 8
                            if idx >= 8:
                                sync.wait_ge(s_mmw.h, mmw0 + (idx - 8) // 4 + 1)
                            s_lw.inc(sync.dma_start(
                                out=WBH[:, slot * 512:(slot + 1) * 512],
                                in_=scr[ob, c, 0]), 16)
                            s_lw.inc(sync.dma_start(
                                out=WBL[:, slot * 512:(slot + 1) * 512],
                                in_=scr[ob, c, 1]), 16)
                    if layer == 2:
                        for ob in range(NOB):
                            for tt in range(NTT):
                                sync.wait_ge(s_ev.h, ev0 + ob * 4 + tt + 1)
                                r0 = g * TG + tt * 128
                                s_out.inc(sync.dma_start(
                                    out=out[r0:r0 + 128, ob * 512:(ob + 1) * 512],
                                    in_=X[:, tt * D + ob * 512:tt * D + ob * 512 + 512]), 16)

                if mode == "dequant":
                    @blk.gpsimd
                    def _(gp, dqb0=dqb0):
                        for widx in range(NOB * C):
                            wnb, wc = divmod(widx, C)
                            slot = widx % 8
                            gp.wait_ge(s_dqb.h, dqb0 + widx + 1)
                            s_sw.inc(gp.dma_start(
                                out=w2scr[wnb, wc, 0],
                                in_=WBH[:, slot * 512:(slot + 1) * 512]), 16)
                            s_sw.inc(gp.dma_start(
                                out=w2scr[wnb, wc, 1],
                                in_=WBL[:, slot * 512:(slot + 1) * 512]), 16)
                        gp.wait_ge(s_sw.h, s_sw.n)

                    @blk.scalar
                    def _(scalar, dqa0=dqa0, dqp0=dqp0, ac0=ac0, dqb0=dqb0,
                          mmw0=mmw0, sw0=sw0):
                        for idx in range(NOB * C):
                            p4 = idx % 4
                            slot = idx % 8
                            scalar.wait_ge(s_dqa.h, dqa0 + idx + 1)
                            if idx >= 2:
                                scalar.wait_ge(s_dqp.h, dqp0 + idx - 1)
                            s_ac.inc(scalar.copy(DQF[:, p4 * 512:(p4 + 1) * 512],
                                                 DQN[:, p4 * 512:(p4 + 1) * 512]))
                            scalar.wait_ge(s_dqp.h, dqp0 + idx + 1)
                            if idx >= 8:
                                scalar.wait_ge(s_mmw.h, mmw0 + (idx - 8) // 4 + 1)
                                scalar.wait_ge(s_sw.h, sw0 + 32 * (idx - 7))
                            s_ac.inc(scalar.copy(WBH[:, slot * 512:(slot + 1) * 512],
                                                 FPS[:, p4 * 512:(p4 + 1) * 512]))

                if mode == "w1g0":
                    @blk.scalar
                    def _(scalar, lw0=lw0, mmw0=mmw0, sw0=sw0, ac0=ac0):
                        for idx in range(NOB * C):
                            p2 = idx % 2
                            slot = idx % 8
                            scalar.wait_ge(s_lw.h, lw0 + 16 * (idx + 1))
                            if idx >= 8:
                                scalar.wait_ge(s_mmw.h, mmw0 + (idx - 8) // 4 + 1)
                                scalar.wait_ge(s_sw.h, sw0 + 32 * (idx - 7))
                            s_ac.inc(scalar.copy(WBH[:, slot * 512:(slot + 1) * 512],
                                                 FPS[:, p2 * 512:(p2 + 1) * 512]))

                    @blk.gpsimd
                    def _(gp, dqb0=dqb0, sw0=sw0):
                        for widx in range(NOB * C):
                            wnb, wc = divmod(widx, C)
                            slot = widx % 8
                            gp.wait_ge(s_dqb.h, dqb0 + widx + 1)
                            s_sw.inc(gp.dma_start(
                                out=w1scr[wnb, wc, 0],
                                in_=WBH[:, slot * 512:(slot + 1) * 512]), 16)
                            s_sw.inc(gp.dma_start(
                                out=w1scr[wnb, wc, 1],
                                in_=WBL[:, slot * 512:(slot + 1) * 512]), 16)
                        gp.wait_ge(s_sw.h, s_sw.n)

                @blk.tensor
                def _(tensor, layer=layer, mode=mode, lw0=lw0, ev0=ev0,
                      dqb0=dqb0, mmw0=mmw0):
                    for ob in range(NOB):
                        bs = (ob % 2) * 4
                        for c in range(C):
                            idx = ob * C + c
                            slot = idx % 8
                            if c == 0 and ob >= 2:
                                tensor.wait_ge(s_ev.h, ev0 + (ob - 2) * 4 + 4)
                            if mode == "scratch":
                                tensor.wait_ge(s_lw.h, lw0 + 32 * (idx + 1))
                            else:
                                tensor.wait_ge(s_dqb.h, dqb0 + idx + 1)
                            wh = WBH[:, slot * 512:(slot + 1) * 512]
                            wl = WBL[:, slot * 512:(slot + 1) * 512]
                            mmi = None
                            for tt in range(NTT):
                                xh = XHTH[:, c * 512 + tt * 128:c * 512 + tt * 128 + 128]
                                xl = XHTL[:, c * 512 + tt * 128:c * 512 + tt * 128 + 128]
                                mmi = tensor.matmul(bank(bs + tt), xh, wh,
                                                    start=(c == 0), stop=False)
                                mmi = tensor.matmul(bank(bs + tt), xh, wl,
                                                    start=False, stop=False)
                                mmi = tensor.matmul(bank(bs + tt), xl, wh,
                                                    start=False, stop=False)
                            if c % 4 == 3:
                                s_mmw.inc(mmi)
                        for tt in range(NTT):
                            if layer == 2:
                                hh = HGSH[:, tt * 128:tt * 128 + 128]
                                hl = HGSL[:, tt * 128:tt * 128 + 128]
                                zh = ZSH[:, ob * 512:(ob + 1) * 512]
                                zl = ZSL[:, ob * 512:(ob + 1) * 512]
                                tensor.matmul(bank(bs + tt), hh, zh,
                                              start=False, stop=False)
                                tensor.matmul(bank(bs + tt), hh, zl,
                                              start=False, stop=False)
                                s_mm.inc(tensor.matmul(bank(bs + tt), hl, zh,
                                                       start=False, stop=True))
                            else:
                                tensor.matmul(
                                    bank(bs + tt), ONES[:],
                                    B1H[0:1, ob * 512:(ob + 1) * 512],
                                    start=False, stop=False)
                                s_mm.inc(tensor.matmul(
                                    bank(bs + tt), ONES[:],
                                    B1L[0:1, ob * 512:(ob + 1) * 512],
                                    start=False, stop=True))

                @blk.vector
                def _(vector, mode=mode, mm0=mm0, ev0=ev0, pk0=pk0, sb0=sb0,
                      ac0=ac0, dqa0=dqa0, dqp0=dqp0, dqb0=dqb0, sw0=sw0,
                      mmw0=mmw0, lw0=lw0):
                    def emit_evac(ob):
                        bs = (ob % 2) * 4
                        for tt in range(NTT):
                            vector.wait_ge(s_mm.h, mm0 + ob * 4 + tt + 1)
                            dst = X[:, tt * D + ob * 512:tt * D + ob * 512 + 512]
                            s_ev.inc(vector.tensor_tensor(dst, dst, bank(bs + tt),
                                                          op=OP.add))

                    if mode == "scratch":
                        for ob in range(NOB):
                            emit_evac(ob)
                    elif mode == "w1g0":
                        for idx in range(NOB * C):
                            p2 = idx % 2
                            slot = idx % 8
                            vector.wait_ge(s_ac.h, ac0 + idx + 1)
                            s_dqb.inc(vector.tensor_tensor(
                                WBL[:, slot * 512:(slot + 1) * 512],
                                FPS[:, p2 * 512:(p2 + 1) * 512],
                                WBH[:, slot * 512:(slot + 1) * 512],
                                op=OP.subtract))
                            if idx % C == 2 and idx >= C + 2:
                                emit_evac(idx // C - 1)
                        emit_evac(NOB - 1)
                    else:
                        def emit_nib(idx):
                            p4 = idx % 4
                            vector.wait_ge(s_pk.h, pk0 + 16 * (idx + 1))
                            if idx >= 2:
                                vector.wait_ge(s_ac.h, ac0 + 2 * (idx - 2) + 1)
                            s_dqa.inc(vector.tensor_scalar(
                                DQN[:, p4 * 512:(p4 + 1) * 512],
                                PKB[:, p4 * 512:(p4 + 1) * 512],
                                SHV[:], 15,
                                op0=OP.logical_shift_right, op1=OP.bitwise_and))

                        def emit_prodf(idx):
                            p4 = idx % 4
                            vector.wait_ge(s_ac.h, ac0 + 2 * idx + 1)
                            vector.wait_ge(s_sb.h, sb0 + 32 * (idx + 1))
                            if idx >= 2:
                                vector.wait_ge(s_dqb.h, dqb0 + idx - 1)
                            s_dqp.inc(vector.tensor_tensor(
                                FPS[:, p4 * 512:(p4 + 1) * 512],
                                DQF[:, p4 * 512:(p4 + 1) * 512],
                                SBC[:, p4 * 512:(p4 + 1) * 512], op=OP.mult))

                        def emit_prodl(idx):
                            p4 = idx % 4
                            slot = idx % 8
                            vector.wait_ge(s_ac.h, ac0 + 2 * idx + 2)
                            if idx >= 8:
                                vector.wait_ge(s_mmw.h, mmw0 + (idx - 8) // 4 + 1)
                                vector.wait_ge(s_sw.h, sw0 + 32 * (idx - 7))
                            s_dqb.inc(vector.tensor_tensor(
                                WBL[:, slot * 512:(slot + 1) * 512],
                                FPS[:, p4 * 512:(p4 + 1) * 512],
                                WBH[:, slot * 512:(slot + 1) * 512],
                                op=OP.subtract))

                        emit_nib(0)
                        emit_nib(1)
                        for idx in range(NOB * C):
                            emit_prodf(idx)
                            if idx + 2 < NOB * C:
                                emit_nib(idx + 2)
                            emit_prodl(idx)
                            if idx % C == 2 and idx >= C + 2:
                                emit_evac(idx // C - 1)
                        emit_evac(NOB - 1)

    with nc.Block("fin") as blk:
        total_out = s_out.n

        @blk.sync
        def _(sync):
            sync.wait_ge(s_out.h, total_out)

    return nc


def _host_prep(L0_attn_x, L0_attn_weight, L0_attn_bias, L0_ffn_packed_weight,
               L0_ffn_scales, L0_ffn_zeros, L0_ffn_bias):
    x = np.ascontiguousarray(np.asarray(L0_attn_x, dtype=np.float32).reshape(8192, D))
    W1 = np.asarray(L0_attn_weight, dtype=np.float32)
    w1t = np.ascontiguousarray(
        W1.T.reshape(D, NOB, 512).transpose(1, 0, 2).reshape(NOB, C, 128, 512))
    b1r = np.ascontiguousarray(np.asarray(L0_attn_bias, np.float32).reshape(1, D))
    b2r = np.ascontiguousarray(np.asarray(L0_ffn_bias, np.float32).reshape(1, D))
    pk = np.ascontiguousarray(np.asarray(L0_ffn_packed_weight, np.int32))
    pk_u8 = pk.view(np.uint8).reshape(D, D // 2, 4)[:, :, 0]
    pkd = np.ascontiguousarray(np.repeat(pk_u8.T, 2, axis=0))
    sclT = np.ascontiguousarray(np.asarray(L0_ffn_scales, np.float32).T)
    zrT = np.ascontiguousarray(np.asarray(L0_ffn_zeros, np.float32).T)
    import ml_dtypes
    onesr = np.ones((1, 128), ml_dtypes.bfloat16)
    selbig = np.zeros((128, 127), ml_dtypes.bfloat16)
    selbig[0:64, 63] = -1.0
    selbig[64:128, 64] = -1.0
    shv = np.zeros((128, 1), np.uint8)
    shv[1::2] = 4
    idb = np.eye(128, dtype=ml_dtypes.bfloat16)
    shared = dict(w1t=w1t, b1r=b1r, b2r=b2r, pkd=pkd, sclT=sclT, zrT=zrT,
                  onesr=onesr.view(np.uint16), selbig=selbig.view(np.uint16),
                  shv=shv, idb=idb.view(np.uint16))
    xs = x.reshape(NCORES, TPC, D)
    return xs, shared


def kernel(L0_attn_x, L0_attn_weight, L0_attn_bias, L0_ffn_packed_weight,
           L0_ffn_scales, L0_ffn_zeros, L0_ffn_bias):
    xs, shared = _host_prep(L0_attn_x, L0_attn_weight, L0_attn_bias,
                            L0_ffn_packed_weight, L0_ffn_scales,
                            L0_ffn_zeros, L0_ffn_bias)
    nc = build_nc()
    in_maps = [dict(shared, x_in=np.ascontiguousarray(xs[i])) for i in range(NCORES)]
    res = run_bass_kernel_spmd(nc, in_maps, core_ids=list(range(NCORES)))
    full = np.concatenate([r["out"] for r in res.results], axis=0)
    return full.reshape(1, 8192, D).astype(np.float32)
